# version 01
# baseline (speedup 1.0000x reference)
import contextlib
import numpy as np
import ml_dtypes
import concourse.bass as bass
import concourse.mybir as mybir
from concourse.bass_utils import run_bass_kernel_spmd

F32 = mybir.dt.float32
BF16 = mybir.dt.bfloat16
I32 = mybir.dt.int32
AF = mybir.ActivationFunctionType
ALU = mybir.AluOpType
AX = mybir.AxisListType
bf16 = ml_dtypes.bfloat16

ENGINES = ("sync", "tensor", "vector", "scalar", "gpsimd")

B, T, D = 2, 4096, 2048
GT = B * T
TOK = GT // 8
NCHUNK = D // 128


class Prog:

    def __init__(self, nc, stack):
        self.nc = nc
        self.stack = stack
        self.streams = {e: [] for e in ENGINES}
        self.semh = {}
        self.semv = {}

    def sem(self, name):
        if name not in self.semh:
            self.semh[name] = self.stack.enter_context(self.nc.semaphore(name))
            self.semv[name] = 0
        return self.semh[name]

    def val(self, name):
        self.sem(name)
        return self.semv[name]

    def _wait(self, eng, name, v):
        if v <= 0:
            return
        h = self.sem(name)
        self.streams[eng].append(lambda e, h=h, v=v: e.wait_ge(h, v))

    def wait(self, eng, tok):
        if tok is None:
            return
        name, v = tok
        if name == "ser_" + eng:
            return
        self._wait(eng, name, v)

    def wait_all(self, eng, name):
        self._wait(eng, name, self.val(name))

    def op(self, eng, fn, chain=True):
        ser = "ser_" + eng
        h = self.sem(ser)
        if chain and eng != "tensor" and self.semv[ser] > 0:
            self._wait(eng, ser, self.semv[ser])
        self.semv[ser] += 1

        def g(e, fn=fn, h=h):
            fn(e).then_inc(h, 1)
        self.streams[eng].append(g)
        return (ser, self.semv[ser])

    def dma(self, eng, out, in_, sem, **kw):
        h = self.sem(sem)
        self.semv[sem] += 16

        def g(e, o=out, i=in_, h=h, kw=kw):
            e.dma_start(out=o, in_=i, **kw).then_inc(h, 16)
        self.streams[eng].append(g)
        return (sem, self.semv[sem])

    def realize(self):
        with self.nc.Block() as blk:
            @blk.sync
            def _(e):
                for f in self.streams["sync"]:
                    f(e)

            @blk.tensor
            def _(e):
                for f in self.streams["tensor"]:
                    f(e)

            @blk.vector
            def _(e):
                for f in self.streams["vector"]:
                    f(e)

            @blk.scalar
            def _(e):
                for f in self.streams["scalar"]:
                    f(e)

            @blk.gpsimd
            def _(e):
                for f in self.streams["gpsimd"]:
                    f(e)


def build_phase_A(upto=99, dtskip=()):
    nc = bass.Bass("TRN2", target_bir_lowering=False)
    XR = nc.dram_tensor("xr", [TOK, D], F32, kind="ExternalInput").ap()
    WBCT = nc.dram_tensor("wbcT", [D, 128], F32, kind="ExternalInput").ap()
    WDTT = nc.dram_tensor("wdtT", [64, D], F32, kind="ExternalInput").ap()
    BDT = nc.dram_tensor("bdt", [D, 1], F32, kind="ExternalInput").ap()
    W0T = nc.dram_tensor("w0T", [D, D], F32, kind="ExternalInput").ap()
    B0 = nc.dram_tensor("b0", [D, 1], F32, kind="ExternalInput").ap()
    IDB = nc.dram_tensor("identb", [128, 128], BF16, kind="ExternalInput").ap()

    XN = nc.dram_tensor("xn", [TOK, D], F32, kind="ExternalOutput").ap()
    DELTA = nc.dram_tensor("delta", [D, TOK], F32, kind="ExternalOutput").ap()
    BCO = nc.dram_tensor("bc", [64, TOK], F32, kind="ExternalOutput").ap()
    FFN0 = nc.dram_tensor("ffn0", [D, TOK], F32, kind="ExternalOutput").ap()
    TOTS = nc.dram_tensor("tots", [1, 1], F32, kind="Internal").ap()

    NT = TOK // 128

    with contextlib.ExitStack() as stack:
        sb = lambda name, shape, dt: stack.enter_context(nc.sbuf_tensor(name, shape, dt))
        ps = lambda name, shape, dt: stack.enter_context(nc.psum_tensor(name, shape, dt))

        xbuf = [sb(f"xbuf{i}", [128, D], F32) for i in range(2)]
        xnf = sb("xnf", [128, D], F32)
        xnb = sb("xnb", [128, D], BF16)
        xnT = sb("xnT", [128, NCHUNK * TOK], BF16)
        ident = sb("ident", [128, 128], BF16)
        mu = sb("mu", [128, 1], F32)
        mur = sb("mur", [128, 1], F32)
        sqa = sb("sqa", [128, 1], F32)
        var = sb("var", [128, 1], F32)
        sd = sb("sd", [128, 1], F32)
        rstd = sb("rstd", [128, 1], F32)
        t1 = sb("t1", [128, 1], F32)
        epsc = sb("epsc", [128, 1], F32)
        wbc_sb = sb("wbc_sb", [128, NCHUNK * 128], BF16)
        wdt_sb = sb("wdt_sb", [64, D], BF16)
        bdt_sb = sb("bdt_sb", [128, NCHUNK], F32)
        b0_sb = sb("b0_sb", [128, NCHUNK], F32)
        dbc = sb("dbc", [128, TOK], F32)
        dtrb = sb("dtrb", [64, TOK], BF16)
        spm = sb("spm", [128, TOK], F32)
        aa = sb("aa", [128, TOK], F32)
        dlt = sb("dlt", [128, TOK], F32)
        wld = [sb(f"wld{i}", [128, D], F32) for i in range(2)]
        absacc = sb("absacc", [128, 1], F32)
        abst = sb("abst", [128, 1], F32)
        ones = sb("ones", [128, 1], F32)
        tot = sb("tot", [1, 1], F32)
        totb = sb("totb", [128, 1], F32)
        thr = sb("thr", [128, 1], F32)
        s0c = sb("s0c", [128, 1], F32)
        awt = sb("awt", [128, D], F32)
        sgt = sb("sgt", [128, D], F32)
        get = sb("get", [128, D], F32)
        w0q = sb("w0q", [128, NCHUNK * D], BF16)
        ffb = sb("ffb", [128, TOK], F32)

        pst = [ps(f"pst{i}", [128, 128], BF16) for i in range(2)]
        psA = ps("psA", [128, 512], F32)
        psB = ps("psB", [128, 512], F32)
        ptot = ps("ptot", [1, 1], F32)

        p = Prog(nc, stack)

        p.dma("sync", ident[:], IDB, sem="ldc")
        for c in range(NCHUNK):
            p.dma("gpsimd", wbc_sb[:, c * 128:(c + 1) * 128],
                  WBCT[c * 128:(c + 1) * 128, :], sem="ldcg")
        p.dma("gpsimd", wdt_sb[:], WDTT, sem="ldcg")
        for c in range(NCHUNK):
            p.dma("sync", bdt_sb[:, c:c + 1], BDT[c * 128:(c + 1) * 128, :], sem="ldc")
            p.dma("sync", b0_sb[:, c:c + 1], B0[c * 128:(c + 1) * 128, :], sem="ldc")
        p.op("vector", lambda e: e.memset(epsc[:], 1e-5))

        t_xn = {}
        t_stxn = {}
        t_cp = {}
        ev_tok = {}
        for i in range(NT):
            xb = xbuf[i % 2]
            if i >= 2:
                p.wait("sync", t_xn[i - 2])
            lx = f"ld_x{i % 2}"
            ldtok = p.dma("sync", xb[:], XR[i * 128:(i + 1) * 128, :], sem=lx)

            p.wait("vector", ldtok)
            p.op("vector", lambda e, xb=xb: e.reduce_sum(mur[:], xb[:], axis=AX.X))
            p.op("vector", lambda e: e.tensor_scalar_mul(mu[:], mur[:], 1.0 / D))
            p.wait("scalar", ldtok)
            if i >= 1:
                p.wait("scalar", t_stxn[i - 1])
                p.wait("scalar", t_cp[i - 1])
            tsq = p.op("scalar", lambda e, xb=xb: e.activation(
                xnf[:], xb[:], AF.Square, accum_out=sqa[:]))
            p.op("vector", lambda e: e.tensor_mul(t1[:], mu[:], mu[:]))
            p.wait("vector", tsq)
            p.op("vector", lambda e: e.tensor_scalar_mul(var[:], sqa[:], 1.0 / D))
            tvar = p.op("vector", lambda e: e.tensor_sub(var[:], var[:], t1[:]))
            p.wait("scalar", tvar)
            tsd = p.op("scalar", lambda e: e.activation(sd[:], var[:], AF.Sqrt,
                                                        bias=epsc[:]))
            p.wait("vector", tsd)
            p.op("vector", lambda e: e.reciprocal(rstd[:], sd[:]))
            if i >= 1:
                p.wait("vector", t_stxn[i - 1])
            t_xn[i] = p.op("vector", lambda e, xb=xb: e.scalar_tensor_tensor(
                xnf[:], xb[:], mu[:], rstd[:].broadcast_to((128, D)),
                ALU.subtract, ALU.mult))
            p.wait("gpsimd", t_xn[i])
            t_stxn[i] = p.dma("gpsimd", XN[i * 128:(i + 1) * 128, :], xnf[:], sem="st_xn")
            tcp = t_cp[i] = p.op("vector", lambda e: e.tensor_copy(xnb[:], xnf[:]))
            p.wait_all("tensor", "ldc")
            p.wait_all("tensor", "ldcg")
            p.wait("tensor", tcp)
            for c in range(NCHUNK):
                g = 16 * i + c
                pt = pst[c % 2]
                if g >= 2:
                    p.wait("tensor", ev_tok[g - 2])
                ttp = p.op("tensor", lambda e, pt=pt, c=c: e.transpose(
                    pt[:], xnb[:, c * 128:(c + 1) * 128], ident[:]))
                p.wait("vector", ttp)
                ev_tok[g] = p.op("vector", lambda e, pt=pt, c=c, i=i: e.tensor_copy(
                    xnT[:, c * TOK + i * 128: c * TOK + (i + 1) * 128], pt[:]))

        if upto < 2:
            p.wait_all("gpsimd", "st_xn")
            p.realize()
            return nc
        p.wait("tensor", ev_tok[16 * NT - 1])
        p.wait("tensor", ev_tok[16 * NT - 2])
        mmtok = {}
        for j in range(2):
            pp = psA if j == 0 else psB
            for c in range(NCHUNK):
                mmtok[j] = p.op("tensor", lambda e, pp=pp, c=c, j=j: e.matmul(
                    pp[:], wbc_sb[:, c * 128:(c + 1) * 128],
                    xnT[:, c * TOK + j * 512: c * TOK + j * 512 + 512],
                    start=(c == 0), stop=(c == NCHUNK - 1)))
        ev_dbc = {}
        for j in range(2):
            pp = psA if j == 0 else psB
            p.wait("vector", mmtok[j])
            ev_dbc[j] = p.op("vector", lambda e, pp=pp, j=j: e.tensor_copy(
                dbc[:, j * 512:(j + 1) * 512], pp[:]))
        tdtr = p.op("vector", lambda e: e.tensor_copy(dtrb[:], dbc[0:64, :]))
        p.wait("gpsimd", ev_dbc[0])
        p.wait("gpsimd", ev_dbc[1])
        t_stbc = p.dma("gpsimd", BCO, dbc[64:128, :], sem="st_bc")

        if upto < 2.5:
            p.wait_all("gpsimd", "st_xn")
            p.wait_all("gpsimd", "st_bc")
            p.realize()
            return nc
        p.wait("tensor", tdtr)
        sp_v = {}
        sp_a = {}
        sp_dl = {}
        st_dl = {}
        nchunk_dt = 1 if upto < 2.7 else NCHUNK
        e_ts = "e_ts" not in dtskip
        e_sc = "e_sc" not in dtskip
        e_add = ("e_add" not in dtskip) and e_ts and e_sc
        e_st = ("e_st" not in dtskip) and e_add

        for n in range(nchunk_dt):
            for j in range(2):
                m = 2 * n + j
                pp = psA if m % 2 == 0 else psB
                if m >= 2:
                    if e_ts:
                        p.wait("tensor", sp_v[m - 2])
                    if e_sc:
                        p.wait("tensor", sp_a[m - 2])
                tmm = p.op("tensor", lambda e, pp=pp, n=n, j=j: e.matmul(
                    pp[:], wdt_sb[:, n * 128:(n + 1) * 128],
                    dtrb[:, j * 512:(j + 1) * 512], start=True, stop=True))
                if e_ts:
                    p.wait("vector", tmm)
                    if n >= 1 and j == 0 and e_st:
                        p.wait("vector", st_dl[n - 1])
                    sp_v[m] = p.op("vector", lambda e, pp=pp, n=n, j=j: e.tensor_scalar(
                        spm[:, j * 512:(j + 1) * 512], pp[:], bdt_sb[:, n:n + 1], 0.0,
                        ALU.add, ALU.max))
                if e_sc:
                    p.wait("scalar", tmm)
                    if e_ts:
                        p.wait("scalar", sp_v[m])
                    if m >= 2 and e_add:
                        p.wait("scalar", sp_dl[m - 2])
                    sp_a[m] = p.op("scalar", lambda e, pp=pp, n=n, j=j: e.activation(
                        aa[:, j * 512:(j + 1) * 512], pp[:], AF.Abs,
                        bias=bdt_sb[:, n:n + 1]))
                    p.op("scalar", lambda e, j=j: e.activation(
                        aa[:, j * 512:(j + 1) * 512], aa[:, j * 512:(j + 1) * 512],
                        AF.Exp, scale=-1.0))
                    tln = p.op("scalar", lambda e, j=j: e.activation(
                        aa[:, j * 512:(j + 1) * 512], aa[:, j * 512:(j + 1) * 512],
                        AF.Ln, bias=1.0))
                if e_add:
                    p.wait("vector", tln)
                    sp_dl[m] = p.op("vector", lambda e, j=j: e.tensor_add(
                        dlt[:, j * 512:(j + 1) * 512], spm[:, j * 512:(j + 1) * 512],
                        aa[:, j * 512:(j + 1) * 512]))
            if e_st:
                p.wait("gpsimd", sp_dl[2 * n])
                p.wait("gpsimd", sp_dl[2 * n + 1])
                st_dl[n] = p.dma("gpsimd", DELTA[n * 128:(n + 1) * 128, :], dlt[:],
                                 sem="st_dl")

        if upto < 3:
            p.wait_all("gpsimd", "st_xn")
            p.wait_all("gpsimd", "st_bc")
            p.wait_all("gpsimd", "st_dl")
            p.realize()
            return nc
        p.op("vector", lambda e: e.memset(ones[:], 1.0))
        w_used = {}
        for c in range(NCHUNK):
            wb = wld[c % 2]
            if c >= 2:
                p.wait("sync", w_used[c - 2])
            lw = f"ld_w{c % 2}"
            lwt = p.dma("sync", wb[:], W0T[c * 128:(c + 1) * 128, :], sem=lw)
            p.wait("vector", lwt)
            if c == 0:
                w_used[c] = p.op("vector", lambda e, wb=wb: e.reduce_sum(
                    absacc[:], wb[:], axis=AX.X, apply_absolute_value=True))
            else:
                p.op("vector", lambda e, wb=wb: e.reduce_sum(
                    abst[:], wb[:], axis=AX.X, apply_absolute_value=True))
                w_used[c] = p.op("vector", lambda e: e.tensor_add(
                    absacc[:], absacc[:], abst[:]))
        p.wait("tensor", w_used[NCHUNK - 1])
        tmt = p.op("tensor", lambda e: e.matmul(ptot[:], ones[:], absacc[:],
                                                start=True, stop=True))
        p.wait("vector", tmt)
        tev = p.op("vector", lambda e: e.tensor_copy(tot[:], ptot[:]))
        p.wait("gpsimd", tev)
        tst = p.dma("gpsimd", TOTS, tot[:], sem="st_tot")
        p.wait("gpsimd", tst)
        tld = p.dma("gpsimd", totb[:], TOTS.broadcast_to((128, 1)), sem="ld_tot")
        p.wait("vector", tld)
        p.op("vector", lambda e: e.tensor_scalar_mul(thr[:], totb[:], 0.5 / (D * D)))
        ts0 = p.op("vector", lambda e: e.tensor_scalar_mul(s0c[:], totb[:], 1.0 / (D * D)))

        q_sg = {}
        q_ge = {}
        q_wq = {}
        for c in range(NCHUNK):
            wb = wld[c % 2]
            if c >= 2:
                p.wait("sync", q_sg[c - 2])
            else:
                p.wait("sync", w_used[NCHUNK - 2 + c])
            lw = f"ld_w{c % 2}"
            lwt = p.dma("sync", wb[:], W0T[c * 128:(c + 1) * 128, :], sem=lw)
            p.wait("scalar", lwt)
            if c >= 1:
                p.wait("scalar", q_ge[c - 1])
                p.wait("scalar", q_wq[c - 1])
            p.op("scalar", lambda e, wb=wb: e.activation(awt[:], wb[:], AF.Abs))
            q_sg[c] = p.op("scalar", lambda e, wb=wb: e.activation(sgt[:], wb[:], AF.Sign))
            p.wait("vector", q_sg[c])
            q_ge[c] = p.op("vector", lambda e: e.tensor_single_scalar(
                get[:], awt[:], thr[:], ALU.is_ge))
            q_wq[c] = p.op("vector", lambda e, c=c: e.tensor_mul(
                w0q[:, c * D:(c + 1) * D], get[:], sgt[:]))

        p.wait("tensor", q_wq[NCHUNK - 1])
        ff_ev = {}
        st_ff = {}
        for n in range(NCHUNK):
            for j in range(2):
                m = 2 * n + j
                pp = psA if m % 2 == 0 else psB
                if m >= 2:
                    p.wait("tensor", ff_ev[m - 2])
                for c in range(NCHUNK):
                    tmm = p.op("tensor", lambda e, pp=pp, n=n, j=j, c=c: e.matmul(
                        pp[:], w0q[:, c * D + n * 128: c * D + (n + 1) * 128],
                        xnT[:, c * TOK + j * 512: c * TOK + j * 512 + 512],
                        start=(c == 0), stop=(c == NCHUNK - 1)))
                p.wait("vector", tmm)
                if n >= 1 and j == 0:
                    p.wait("vector", st_ff[n - 1])
                ff_ev[m] = p.op("vector", lambda e, pp=pp, n=n, j=j: e.scalar_tensor_tensor(
                    ffb[:, j * 512:(j + 1) * 512], pp[:], s0c[:],
                    b0_sb[:, n:n + 1].broadcast_to((128, 512)),
                    ALU.mult, ALU.add))
            p.wait("gpsimd", ff_ev[2 * n])
            p.wait("gpsimd", ff_ev[2 * n + 1])
            st_ff[n] = p.dma("gpsimd", FFN0[n * 128:(n + 1) * 128, :], ffb[:], sem="st_ff")

        p.wait_all("gpsimd", "st_xn")
        p.wait_all("gpsimd", "st_bc")
        p.wait_all("gpsimd", "st_dl")
        p.wait_all("gpsimd", "st_ff")

        p.realize()
    return nc


def _phase_A_inmaps(inputs):
    x2d = np.ascontiguousarray(inputs["L0_attn_x"].reshape(GT, D))
    wbcT = np.ascontiguousarray(inputs["L0_attn_W_bc"].T)
    wdtT = np.ascontiguousarray(inputs["L0_attn_W_dt"].T)
    bdt = np.ascontiguousarray(inputs["L0_attn_b_dt"].reshape(D, 1))
    w0T = np.ascontiguousarray(inputs["L0_ffn_weight"].T)
    b0 = np.ascontiguousarray(inputs["L0_ffn_bias"].reshape(D, 1))
    idb = np.eye(128, dtype=bf16)
    return [{"xr": x2d[c * TOK:(c + 1) * TOK], "wbcT": wbcT, "wdtT": wdtT,
             "bdt": bdt, "w0T": w0T, "b0": b0, "identb": idb} for c in range(8)]


def run_phase_A(inputs):
    nc = build_phase_A()
    res = run_bass_kernel_spmd(nc, _phase_A_inmaps(inputs), core_ids=list(range(8)))
    xn = np.concatenate([r["xn"] for r in res.results], axis=0)
    delta = np.concatenate([r["delta"] for r in res.results], axis=1)
    bc = np.concatenate([r["bc"] for r in res.results], axis=1)
    ffn0 = np.concatenate([r["ffn0"] for r in res.results], axis=1)
    return xn, delta, bc, ffn0




TC_B = 128
NCH_B = T // TC_B
NB_B = 4


def build_phase_B():
    nc = bass.Bass("TRN2", target_bir_lowering=False)
    DL = nc.dram_tensor("dl", [512, T], F32, kind="ExternalInput").ap()
    U = nc.dram_tensor("u", [512, T], F32, kind="ExternalInput").ap()
    BT = nc.dram_tensor("bt", [T, 32], F32, kind="ExternalInput").ap()
    CT = nc.dram_tensor("ct", [T, 32], F32, kind="ExternalInput").ap()
    ALOG = nc.dram_tensor("alog", [512, 32], F32, kind="ExternalInput").ap()
    DPAR = nc.dram_tensor("dpar", [512, 1], F32, kind="ExternalInput").ap()
    Y = nc.dram_tensor("y", [512, T], F32, kind="ExternalOutput").ap()

    F = TC_B * 32

    with contextlib.ExitStack() as stack:
        sb = lambda name, shape, dt: stack.enter_context(nc.sbuf_tensor(name, shape, dt))

        dlb = [sb(f"dlb{i}", [128, TC_B], F32) for i in range(2)]
        ub = [sb(f"ub{i}", [128, TC_B], F32) for i in range(2)]
        du = sb("du", [128, TC_B], F32)
        dA = sb("dA", [128, F], F32)
        dBu = sb("dBu", [128, F], F32)
        hh = sb("hh", [128, F], F32)
        bbc = [sb(f"bbc{i}", [128, F], F32) for i in range(2)]
        cbc = [sb(f"cbc{i}", [128, F], F32) for i in range(2)]
        yb = [sb(f"yb{i}", [128, TC_B], F32) for i in range(2)]
        yr = sb("yr", [128, TC_B], F32)
        negA = sb("negA", [128, NB_B * 32], F32)
        dcol = sb("dcol", [128, NB_B], F32)
        state = sb("state", [128, NB_B * 32], F32)

        p = Prog(nc, stack)

        for b in range(NB_B):
            tl = p.dma("sync", negA[:, b * 32:(b + 1) * 32],
                       ALOG[b * 128:(b + 1) * 128, :], sem=f"ld_al{b}")
            p.wait("scalar", tl)
            texp = p.op("scalar", lambda e, b=b: e.activation(
                negA[:, b * 32:(b + 1) * 32], negA[:, b * 32:(b + 1) * 32], AF.Exp))
            p.wait("vector", texp)
            tn = p.op("vector", lambda e, b=b: e.tensor_scalar_mul(
                negA[:, b * 32:(b + 1) * 32], negA[:, b * 32:(b + 1) * 32], -1.0))
            p.wait("scalar", tn)
            tdp = p.dma("sync", dcol[:, b:b + 1], DPAR[b * 128:(b + 1) * 128, :],
                        sem=f"ld_dp{b}")
            p.wait("vector", tdp)
        tst = p.op("vector", lambda e: e.memset(state[:], 0.0))

        def bc_src(M, c):
            sl = M[c * TC_B:(c + 1) * TC_B, :]
            return sl.rearrange("t n -> (t n)").unsqueeze(0).broadcast_to((128, F))

        tb = {0: p.dma("sync", bbc[0][:], bc_src(BT, 0), sem="ld_b0")}
        tc = {0: p.dma("sync", cbc[0][:], bc_src(CT, 0), sem="ld_c0")}

        dbu_tok = {}
        ym_tok = {}
        yfin = {}
        scan_last = {}
        st_y = {}
        exp_last = {}

        for c in range(NCH_B):
            if c + 1 < NCH_B:
                if c >= 1:
                    for b in range(NB_B):
                        p.wait("sync", dbu_tok[(c - 1, b)])
                tb[c + 1] = p.dma("sync", bbc[(c + 1) % 2][:], bc_src(BT, c + 1),
                                  sem=f"ld_b{(c + 1) % 2}")
                if c >= 1:
                    for b in range(NB_B):
                        p.wait("sync", ym_tok[(c - 1, b)])
                tc[c + 1] = p.dma("sync", cbc[(c + 1) % 2][:], bc_src(CT, c + 1),
                                  sem=f"ld_c{(c + 1) % 2}")
            for b in range(NB_B):
                it = c * NB_B + b
                if it >= 2:
                    p.wait("sync", yfin[it - 2])
                    p.wait("sync", exp_last[it - 2])
                tdl = p.dma("sync", dlb[it % 2][:],
                            DL[b * 128:(b + 1) * 128, c * TC_B:(c + 1) * TC_B],
                            sem=f"ld_dl{it % 2}")
                tu = p.dma("sync", ub[it % 2][:],
                           U[b * 128:(b + 1) * 128, c * TC_B:(c + 1) * TC_B],
                           sem=f"ld_u{it % 2}")
                p.wait("vector", tdl)
                p.wait("vector", tu)
                tdu = p.op("vector", lambda e, it=it: e.tensor_mul(
                    du[:], dlb[it % 2][:], ub[it % 2][:]))
                p.wait("scalar", tdl)
                if it >= 1:
                    p.wait("scalar", scan_last[it - 1])
                dA3 = dA.ap().rearrange("p (t n) -> p t n", n=32)
                for n in range(32):
                    exp_last[it] = p.op("scalar", lambda e, n=n, b=b, it=it: e.activation(
                        dA3[:, :, n], dlb[it % 2][:], AF.Exp,
                        scale=negA[:, b * 32 + n:b * 32 + n + 1]))
                p.wait("vector", tb[c])
                dbu_tok[(c, b)] = p.op("vector", lambda e, c=c: e.tensor_mul(
                    dBu[:], du.ap().unsqueeze(2).broadcast_to((128, TC_B, 32)),
                    bbc[c % 2][:].rearrange("p (t n) -> p t n", n=32)))
                p.wait("vector", exp_last[it])
                dBu3 = dBu.ap().rearrange("p (t n) -> p t n", n=32)
                hh3 = hh.ap().rearrange("p (t n) -> p t n", n=32)
                for n in range(32):
                    scan_last[it] = p.op("vector", lambda e, n=n, b=b: e.tensor_tensor_scan(
                        hh3[:, :, n], dA3[:, :, n], dBu3[:, :, n],
                        state[:, b * 32 + n:b * 32 + n + 1], ALU.mult, ALU.add))
                p.op("vector", lambda e, b=b: e.tensor_copy(
                    state[:, b * 32:(b + 1) * 32], hh[:, (TC_B - 1) * 32:TC_B * 32]))
                p.wait("vector", tc[c])
                ym_tok[(c, b)] = p.op("vector", lambda e, c=c: e.tensor_mul(
                    hh[:], hh[:], cbc[c % 2][:]))
                p.op("vector", lambda e: e.reduce_sum(
                    yr[:], hh.ap().rearrange("p (t n) -> p t n", n=32), axis=AX.X))
                if it >= 2:
                    p.wait("vector", st_y[it - 2])
                yfin[it] = p.op("vector", lambda e, it=it, b=b: e.scalar_tensor_tensor(
                    yb[it % 2][:], ub[it % 2][:], dcol[:, b:b + 1], yr[:],
                    ALU.mult, ALU.add))
                p.wait("gpsimd", yfin[it])
                st_y[it] = p.dma("gpsimd",
                                 Y[b * 128:(b + 1) * 128, c * TC_B:(c + 1) * TC_B],
                                 yb[it % 2][:], sem=f"st_y{it % 2}")

        p.wait_all("gpsimd", "st_y0")
        p.wait_all("gpsimd", "st_y1")
        p.realize()
    return nc


def _phase_B_inmaps(inputs, xn, delta, bc):
    maps = []
    for core in range(8):
        b = core // 4
        d0 = (core % 4) * 512
        uT = np.ascontiguousarray(xn[b * T:(b + 1) * T, d0:d0 + 512].T)
        dl = np.ascontiguousarray(delta[d0:d0 + 512, b * T:(b + 1) * T])
        bt = np.ascontiguousarray(bc[0:32, b * T:(b + 1) * T].T)
        ct = np.ascontiguousarray(bc[32:64, b * T:(b + 1) * T].T)
        alog = np.ascontiguousarray(inputs["L0_attn_A_log"][d0:d0 + 512, :])
        dpar = np.ascontiguousarray(inputs["L0_attn_D_param"][d0:d0 + 512].reshape(512, 1))
        maps.append({"dl": dl, "u": uT, "bt": bt, "ct": ct, "alog": alog,
                     "dpar": dpar})
    return maps


def run_phase_B(inputs, xn, delta, bc):
    nc = build_phase_B()
    res = run_bass_kernel_spmd(nc, _phase_B_inmaps(inputs, xn, delta, bc),
                               core_ids=list(range(8)))
    y = np.zeros((2, 2048, T), np.float32)
    for core in range(8):
        b = core // 4
        d0 = (core % 4) * 512
        y[b, d0:d0 + 512] = res.results[core]["y"]
    return y




def build_phase_C():
    nc = bass.Bass("TRN2", target_bir_lowering=False)
    XR = nc.dram_tensor("xr", [TOK, D], F32, kind="ExternalInput").ap()
    YR = nc.dram_tensor("yr", [TOK, D], F32, kind="ExternalInput").ap()
    F0 = nc.dram_tensor("f0", [TOK, D], F32, kind="ExternalInput").ap()
    HLX = nc.dram_tensor("hlx", [3, D], F32, kind="ExternalInput").ap()
    WT = {s: nc.dram_tensor(f"w{s}T", [D, D], F32, kind="ExternalInput").ap()
          for s in "rkvg"}
    TM = {s: nc.dram_tensor(f"tm{s}", [D, 1], F32, kind="ExternalInput").ap()
          for s in "rkvg"}
    IDB = nc.dram_tensor("identb", [128, 128], BF16, kind="ExternalInput").ap()

    X1 = nc.dram_tensor("x1", [TOK, D], F32, kind="ExternalOutput").ap()
    XN1T = nc.dram_tensor("xn1T", [D, TOK], BF16, kind="ExternalOutput").ap()
    OUT = {s: nc.dram_tensor(f"o{s}", [D, TOK], F32, kind="ExternalOutput").ap()
           for s in "rkvg"}
    HSC = nc.dram_tensor("hsc", [D, 1], BF16, kind="Internal").ap()

    NT = TOK // 128
    TOKH = TOK + 1

    with contextlib.ExitStack() as stack:
        sb = lambda name, shape, dt: stack.enter_context(nc.sbuf_tensor(name, shape, dt))
        ps = lambda name, shape, dt: stack.enter_context(nc.psum_tensor(name, shape, dt))

        xb = [sb(f"xb{i}", [128, D], F32) for i in range(2)]
        yb = [sb("ybc0", [128, D], F32)] * 2
        fb = [sb("fbc0", [128, D], F32)] * 2
        x1f = sb("x1f", [128, D], F32)
        x1b = sb("x1b", [128, D], BF16)
        xnT = sb("xnTc", [128, NCHUNK * TOKH], BF16)
        ident = sb("identc", [128, 128], BF16)
        mu = sb("muc", [128, 1], F32)
        mur = sb("murc", [128, 1], F32)
        sqa = sb("sqac", [128, 1], F32)
        var = sb("varc", [128, 1], F32)
        sd = sb("sdc", [128, 1], F32)
        rstd = sb("rstdc", [128, 1], F32)
        t1 = sb("t1c", [128, 1], F32)
        epsc = sb("epscc", [128, 1], F32)
        h1 = sb("h1", [1, D], F32)
        h1b = sb("h1b", [1, D], BF16)
        hmu = sb("hmu", [1, 1], F32)
        hvar = sb("hvar", [1, 1], F32)
        hsd = sb("hsd", [1, 1], F32)
        hrs = sb("hrs", [1, 1], F32)
        ht1 = sb("ht1", [1, 1], F32)
        tmt = sb("tmt", [128, 4 * NCHUNK], F32)
        omt = sb("omt", [128, 4 * NCHUNK], F32)
        xsb = sb("xsb", [128, NCHUNK * TOK], BF16)
        xtmp = sb("xtmp", [128, TOK], F32)
        wsb = sb("wsb", [128, NCHUNK * D], BF16)
        ob = sb("ob", [128, TOK], F32)

        pst = [ps(f"pstc{i}", [128, 128], BF16) for i in range(2)]
        psA = ps("psAc", [128, 512], F32)
        psB = ps("psBc", [128, 512], F32)

        p = Prog(nc, stack)

        p.dma("sync", ident[:], IDB, sem="ldc")
        for si, s in enumerate("rkvg"):
            for c in range(NCHUNK):
                p.dma("sync", tmt[:, si * NCHUNK + c:si * NCHUNK + c + 1],
                      TM[s][c * 128:(c + 1) * 128, :], sem="ldc")
        tmv = p.op("vector", lambda e: e.memset(epsc[:], 1e-5))
        p.wait_all("vector", "ldc")
        tom = p.op("vector", lambda e: e.tensor_scalar(
            omt[:], tmt[:], -1.0, 1.0, ALU.mult, ALU.add))

        th0 = p.dma("gpsimd", h1[:], HLX[0:1, :], sem="ld_h")
        p.wait("gpsimd", th0)
        th0 = p.dma("gpsimd", h1[:], HLX[1:2, :], sem="ld_h", accum_op=ALU.add)
        p.wait("gpsimd", th0)
        th1 = p.dma("gpsimd", h1[:], HLX[2:3, :], sem="ld_h", accum_op=ALU.add)
        p.wait("vector", th1)
        p.op("vector", lambda e: e.reduce_sum(hmu[:], h1[:], axis=AX.X))
        p.op("vector", lambda e: e.tensor_scalar_mul(hmu[:], hmu[:], 1.0 / D))
        p.wait("scalar", th1)
        tsq = p.op("scalar", lambda e: e.activation(x1f[0:1, :], h1[:], AF.Square,
                                                    accum_out=hvar[:]))
        p.wait("vector", tsq)
        p.op("vector", lambda e: e.tensor_scalar_mul(hvar[:], hvar[:], 1.0 / D))
        p.op("vector", lambda e: e.tensor_mul(ht1[:], hmu[:], hmu[:]))
        tv = p.op("vector", lambda e: e.tensor_sub(hvar[:], hvar[:], ht1[:]))
        p.wait("scalar", tv)
        tsd = p.op("scalar", lambda e: e.activation(hsd[:], hvar[:], AF.Sqrt,
                                                    bias=epsc[0:1, :]))
        p.wait("vector", tsd)
        p.op("vector", lambda e: e.reciprocal(hrs[:], hsd[:]))
        thn = p.op("vector", lambda e: e.scalar_tensor_tensor(
            h1b[:], h1[:], hmu[:], hrs[:].broadcast_to((1, D)),
            ALU.subtract, ALU.mult))
        p.wait("gpsimd", thn)
        tst = p.dma("gpsimd", HSC.rearrange("d one -> one d"), h1b[:], sem="st_h")
        p.wait("sync", tst)
        for c in range(NCHUNK):
            p.dma("sync", xnT[:, c * TOKH:c * TOKH + 1],
                  HSC[c * 128:(c + 1) * 128, :], sem="ld_hb")

        t_x1 = {}
        t_st = {}
        t_cp = {}
        ev_tok = {}
        t_sq = {}
        for i in range(NT):
            if i >= 2:
                p.wait("sync", t_x1[i - 2])
            lx = f"ld_x{i % 2}"
            tx = p.dma("sync", xb[i % 2][:], XR[i * 128:(i + 1) * 128, :], sem=lx)
            if i >= 1:
                p.wait("sync", t_x1[i - 1])
                p.wait("sync", t_sq[i - 1])
            ty = p.dma("sync", yb[0][:], YR[i * 128:(i + 1) * 128, :], sem="ld_y0")
            tf = p.dma("sync", fb[0][:], F0[i * 128:(i + 1) * 128, :], sem="ld_f0")
            for tk in (tx, ty, tf):
                p.wait("vector", tk)
            if i >= 1:
                p.wait("vector", t_st[i - 1])
                p.wait("vector", t_cp[i - 1])
            p.op("vector", lambda e, i=i: e.tensor_add(x1f[:], xb[i % 2][:], yb[i % 2][:]))
            t_x1[i] = p.op("vector", lambda e, i=i: e.tensor_add(
                x1f[:], x1f[:], fb[i % 2][:]))
            p.wait("gpsimd", t_x1[i])
            t_st[i] = p.dma("gpsimd", X1[i * 128:(i + 1) * 128, :], x1f[:], sem="st_x1")
            p.op("vector", lambda e: e.reduce_sum(mur[:], x1f[:], axis=AX.X))
            p.op("vector", lambda e: e.tensor_scalar_mul(mu[:], mur[:], 1.0 / D))
            p.wait("scalar", t_x1[i])
            if i >= 1:
                p.wait("scalar", t_cp[i - 1])
            tsq = t_sq[i] = p.op("scalar", lambda e, i=i: e.activation(
                fb[i % 2][:], x1f[:], AF.Square, accum_out=sqa[:]))
            p.op("vector", lambda e: e.tensor_mul(t1[:], mu[:], mu[:]))
            p.wait("vector", tsq)
            p.op("vector", lambda e: e.tensor_scalar_mul(var[:], sqa[:], 1.0 / D))
            tvar = p.op("vector", lambda e: e.tensor_sub(var[:], var[:], t1[:]))
            p.wait("scalar", tvar)
            tsd = p.op("scalar", lambda e: e.activation(sd[:], var[:], AF.Sqrt,
                                                        bias=epsc[:]))
            p.wait("vector", tsd)
            p.op("vector", lambda e: e.reciprocal(rstd[:], sd[:]))
            t_cp[i] = p.op("vector", lambda e, i=i: e.scalar_tensor_tensor(
                x1b[:], x1f[:], mu[:], rstd[:].broadcast_to((128, D)),
                ALU.subtract, ALU.mult))
            p.wait("tensor", t_cp[i])
            p.wait_all("tensor", "ldc")
            for c in range(NCHUNK):
                g = 16 * i + c
                pt = pst[c % 2]
                if g >= 2:
                    p.wait("tensor", ev_tok[g - 2])
                ttp = p.op("tensor", lambda e, pt=pt, c=c: e.transpose(
                    pt[:], x1b[:, c * 128:(c + 1) * 128], ident[:]))
                p.wait("vector", ttp)
                ev_tok[g] = p.op("vector", lambda e, pt=pt, c=c, i=i: e.tensor_copy(
                    xnT[:, c * TOKH + 1 + i * 128: c * TOKH + 1 + (i + 1) * 128], pt[:]))

        p.wait_all("vector", "ld_hb")
        for c in range(NCHUNK):
            p.wait("gpsimd", ev_tok[16 * (NT - 1) + c])
            p.dma("gpsimd", XN1T[c * 128:(c + 1) * 128, :],
                  xnT[:, c * TOKH + 1:(c + 1) * TOKH], sem="st_xnT")

        mix_tok = {}
        last_mm = None
        prev_ev = None
        prev_tso = None
        for si, s in enumerate("rkvg"):
            if last_mm is not None:
                p.wait("gpsimd", last_mm)
            for c in range(NCHUNK):
                p.dma("gpsimd", wsb[:, c * D:(c + 1) * D],
                      WT[s][c * 128:(c + 1) * 128, :], sem=f"ld_w{s}")
            if last_mm is not None:
                p.wait("vector", last_mm)
            for c in range(NCHUNK):
                if si == 0:
                    p.wait("vector", ev_tok[16 * (NT - 1) + c])
                cur = xnT[:, c * TOKH + 1:(c + 1) * TOKH]
                shf = xnT[:, c * TOKH:(c + 1) * TOKH - 1]
                p.op("vector", lambda e, c=c, si=si, cur=cur: e.tensor_single_scalar(
                    xtmp[:], cur, omt[:, si * NCHUNK + c:si * NCHUNK + c + 1], ALU.mult))
                mix_tok[(si, c)] = p.op("vector",
                    lambda e, c=c, si=si, shf=shf: e.scalar_tensor_tensor(
                        xsb[:, c * TOK:(c + 1) * TOK], shf,
                        tmt[:, si * NCHUNK + c:si * NCHUNK + c + 1], xtmp[:],
                        ALU.mult, ALU.add))
            ev_ff = {}
            eng = "scalar" if s == "g" else "vector"
            if prev_tso is not None:
                p.wait(eng, prev_tso)
            for n in range(NCHUNK):
                for j in range(2):
                    m = 2 * n + j
                    pp = psA if m % 2 == 0 else psB
                    if m >= 2:
                        p.wait("tensor", ev_ff[m - 2])
                    elif prev_ev is not None:
                        p.wait("tensor", prev_ev[30 + m])
                    if m == 0:
                        p.wait_all("tensor", f"ld_w{s}")
                        p.wait("tensor", mix_tok[(si, NCHUNK - 1)])
                        p.wait("tensor", mix_tok[(si, NCHUNK - 2)])
                    for c in range(NCHUNK):
                        tmm = p.op("tensor", lambda e, pp=pp, n=n, j=j, c=c: e.matmul(
                            pp[:], wsb[:, c * D + n * 128: c * D + (n + 1) * 128],
                            xsb[:, c * TOK + j * 512: c * TOK + j * 512 + 512],
                            start=(c == 0), stop=(c == NCHUNK - 1)))
                    if s == "g":
                        p.wait("scalar", tmm)
                        ev_ff[m] = p.op("scalar", lambda e, pp=pp, j=j: e.activation(
                            ob[:, j * 512:(j + 1) * 512], pp[:], AF.Sigmoid))
                    else:
                        p.wait("vector", tmm)
                        ev_ff[m] = p.op("vector", lambda e, pp=pp, j=j: e.tensor_copy(
                            ob[:, j * 512:(j + 1) * 512], pp[:]))
                p.wait("gpsimd", ev_ff[2 * n])
                p.wait("gpsimd", ev_ff[2 * n + 1])
                tso = p.dma("gpsimd", OUT[s][n * 128:(n + 1) * 128, :], ob[:],
                            sem=f"st_o{s}")
                p.wait(eng, tso)
            prev_ev = ev_ff
            last_mm = tmm
            prev_tso = tso

        p.wait_all("gpsimd", "st_x1")
        p.wait_all("gpsimd", "st_xnT")
        for s in "rkvg":
            p.wait_all("gpsimd", f"st_o{s}")
        p.realize()
    return nc




def _phase_C_inmaps(inputs, y_rows, f0_rows):
    x2d = np.ascontiguousarray(inputs["L0_attn_x"].reshape(GT, D))
    wT = {s: np.ascontiguousarray(inputs[f"L1_attn_W{s}"].T) for s in "rkvg"}
    tm = {s: np.ascontiguousarray(inputs[f"L1_attn_tm_{s}"].reshape(D, 1))
          for s in "rkvg"}
    idb = np.eye(128, dtype=bf16)
    maps = []
    for c in range(8):
        r0 = c * TOK
        h = r0 + 1 if c in (0, 4) else r0 - 1
        hlx = np.ascontiguousarray(
            np.stack([x2d[h], y_rows[h], f0_rows[h]], axis=0))
        m = {"xr": x2d[r0:r0 + TOK], "yr": np.ascontiguousarray(y_rows[r0:r0 + TOK]),
             "f0": np.ascontiguousarray(f0_rows[r0:r0 + TOK]), "hlx": hlx,
             "identb": idb}
        for s in "rkvg":
            m[f"w{s}T"] = wT[s]
            m[f"tm{s}"] = tm[s]
        maps.append(m)
    return maps


def run_phase_C(inputs, y_rows, f0_rows):
    nc = build_phase_C()
    res = run_bass_kernel_spmd(nc, _phase_C_inmaps(inputs, y_rows, f0_rows),
                               core_ids=list(range(8)))
    x1 = np.concatenate([r["x1"] for r in res.results], axis=0)
    xn1T = np.concatenate([r["xn1T"] for r in res.results], axis=1)
    outs = {s: np.concatenate([r[f"o{s}"] for r in res.results], axis=1)
            for s in "rkvg"}
    return x1, xn1T, outs




LC_D = 16
SBK_D = 512
NSB_D = T // SBK_D
NCL_D = SBK_D // LC_D
LAM_CLAMP = -5.0


def build_phase_D(nsb=NSB_D, ncl=NCL_D, noband=False):
    nc = bass.Bass("TRN2", target_bir_lowering=False)
    RI = nc.dram_tensor("ri", [512, T], F32, kind="ExternalInput").ap()
    KI = nc.dram_tensor("ki", [512, T], F32, kind="ExternalInput").ap()
    VI = nc.dram_tensor("vi", [512, T], F32, kind="ExternalInput").ap()
    GI = nc.dram_tensor("gi", [512, T], F32, kind="ExternalInput").ap()
    WDEC = nc.dram_tensor("wdec", [128, 4], F32, kind="ExternalInput").ap()
    TFIR = nc.dram_tensor("tfir", [128, 4], F32, kind="ExternalInput").ap()
    WOT = nc.dram_tensor("woT", [512, D], F32, kind="ExternalInput").ap()
    IDB = nc.dram_tensor("identb", [128, 128], BF16, kind="ExternalInput").ap()
    IDF = nc.dram_tensor("identf", [128, 128], F32, kind="ExternalInput").ap()
    PY1 = nc.dram_tensor("py1", [D, T], F32, kind="ExternalOutput").ap()
    CDR = nc.dram_tensor("cdr", [1, 4 * 5 * SBK_D], F32, kind="Internal").ap()

    KH = SBK_D + 5

    with contextlib.ExitStack() as stack:
        sb = lambda name, shape, dt: stack.enter_context(nc.sbuf_tensor(name, shape, dt))
        ps = lambda name, shape, dt: stack.enter_context(nc.psum_tensor(name, shape, dt))

        r_sb = sb("r_sb", [128, 4 * SBK_D], F32)
        k_sb = sb("k_sb", [128, 4 * KH], F32)
        v_sb = sb("v_sb", [128, 4 * KH], F32)
        g_sb = sb("g_sb", [128, 4 * SBK_D], F32)
        obig = sb("obig", [128, 4 * SBK_D], F32)
        ocor = sb("ocor", [128, 4 * SBK_D], F32)
        ogb = sb("ogb", [128, 4 * SBK_D], BF16)
        cbc = sb("cbc", [128, 5 * SBK_D], F32)
        mstk = sb("mstk", [128, 5 * SBK_D], F32)
        mtmp = sb("mtmp", [128, SBK_D], F32)
        cvrow = sb("cvrow", [1, SBK_D], F32)
        wo_sb = sb("wo_sb", [128, 4 * D], BF16)
        y1b = sb("y1b", [128, SBK_D], F32)
        ident = sb("identd", [128, 128], BF16)
        idf = sb("idfd", [128, 128], F32)
        ones = sb("onesd", [128, 1], F32)
        lam = sb("lam", [128, 4], F32)
        lamc = sb("lamc", [128, 4], F32)
        nlam = sb("nlam", [128, 4], F32)
        nlamc = sb("nlamc", [128, 4], F32)
        l15 = sb("l15", [128, 4], F32)
        w16 = sb("w16", [128, 4], F32)
        tfir = sb("tfirs", [128, 4], F32)
        pd = sb("pd", [128, 16], F32)
        pdc = sb("pdc", [128, 4], F32)
        it16 = sb("it16", [128, 16], I32)
        mki = sb("mki", [16, 64], I32)
        ski = sb("ski", [16, 64], I32)
        it16f = sb("it16f", [128, 16], F32)
        e1 = sb("e1", [128, 64], F32)
        e1c = sb("e1c", [128, 64], F32)
        e3c = sb("e3c", [128, 64], F32)
        e2 = sb("e2", [128, 64], F32)
        e2t = sb("e2t", [16, 512], F32)
        dlc = sb("dlc", [128, 1], F32)
        rs = sb("rs", [128, 64], F32)
        rc = sb("rc", [128, 64], F32)
        kc = sb("kc", [128, 64], F32)
        kb = sb("kb", [128, 64], F32)
        ktk = sb("ktk", [16, 512], F32)
        vtk = sb("vtk", [16, 512], F32)
        kbt = sb("kbt", [16, 512], F32)
        amask = sb("amask", [16, 64], I32)
        azero = sb("azero", [16, 64], F32)
        asb = sb("asb", [16, 64], F32)
        S_sb = sb("S_sbd", [128, 4 * 128], F32)

        psK = ps("psKd", [16, 512], F32)
        psV = ps("psVd", [16, 512], F32)
        psA16 = ps("psA16", [16, 64], F32)
        psO = ps("psOd", [128, 64], F32)
        psS = ps("psSd", [128, 512], F32)
        psY = [ps(f"psYd{i}", [128, 512], F32) for i in range(2)]
        psC = ps("psCd", [1, 512], F32)

        p = Prog(nc, stack)

        p.dma("sync", ident[:], IDB, sem="ldc")
        p.dma("sync", idf[:], IDF, sem="ldc")
        tw = p.dma("sync", lam[:], WDEC, sem="ldc")
        tf = p.dma("sync", tfir[:], TFIR, sem="ldc")
        for c in range(4):
            p.dma("gpsimd", wo_sb[:, c * D:(c + 1) * D],
                  WOT[c * 128:(c + 1) * 128, :], sem="ldwo")
        p.op("vector", lambda e: e.memset(ones[:], 1.0))
        tmsS = p.op("vector", lambda e: e.memset(S_sb[:], 0.0))
        tmsk = p.op("vector", lambda e: e.memset(k_sb[:], 0.0))
        tmsv = p.op("vector", lambda e: e.memset(v_sb[:], 0.0))
        tit = p.op("gpsimd", lambda e: e.iota(it16[:], pattern=[[1, 16]], base=0,
                                              channel_multiplier=0))
        p.wait("vector", tit)
        p.op("vector", lambda e: e.tensor_copy(it16f[:], it16[:]))
        tmk2 = p.op("gpsimd", lambda e: e.iota(mki[:], pattern=[[0, 4], [1, 16]],
                                               base=0, channel_multiplier=0))
        tsq = p.op("gpsimd", lambda e: e.iota(ski[:], pattern=[[0, 4 * 16]],
                                              base=0, channel_multiplier=1))
        p.wait("vector", tmk2)
        p.wait("vector", tsq)
        p.op("vector", lambda e: e.tensor_tensor(
            amask[:], ski[:], mki[:], ALU.is_lt))
        p.op("vector", lambda e: e.memset(azero[:], 0.0))
        p.wait_all("scalar", "ldc")
        p.wait_all("vector", "ldc")
        tl = p.op("scalar", lambda e: e.activation(lam[:], lam[:], AF.Exp))
        p.wait("vector", tl)
        p.op("vector", lambda e: e.tensor_scalar_mul(lam[:], lam[:], -1.0))
        p.op("vector", lambda e: e.tensor_scalar_max(lamc[:], lam[:], LAM_CLAMP))
        p.op("vector", lambda e: e.tensor_scalar_mul(nlam[:], lam[:], -1.0))
        p.op("vector", lambda e: e.tensor_scalar_mul(nlamc[:], lamc[:], -1.0))
        tl2 = p.op("vector", lambda e: e.tensor_scalar_mul(l15[:], lam[:], 15.0))
        p.wait("scalar", tl2)
        p.op("scalar", lambda e: e.activation(w16[:], lam[:], AF.Exp, scale=16.0))
        for j in range(4):
            p.op("scalar", lambda e, j=j: e.activation(
                e1[:, j * 16:(j + 1) * 16], it16f[:], AF.Exp, scale=lam[:, j:j + 1]))
            p.op("scalar", lambda e, j=j: e.activation(
                e1c[:, j * 16:(j + 1) * 16], it16f[:], AF.Exp, scale=lamc[:, j:j + 1]))
            p.op("scalar", lambda e, j=j: e.activation(
                e3c[:, j * 16:(j + 1) * 16], it16f[:], AF.Exp,
                scale=nlamc[:, j:j + 1], bias=nlamc[:, j:j + 1]))
            tse = p.op("scalar", lambda e, j=j: e.activation(
                e2[:, j * 16:(j + 1) * 16], it16f[:], AF.Exp,
                scale=nlam[:, j:j + 1], bias=l15[:, j:j + 1]))
        for d in range(1, 5):
            tdl = p.op("vector", lambda e, d=d: e.memset(dlc[:], float(d)))
            p.wait("scalar", tdl)
            for j in range(4):
                p.op("scalar", lambda e, j=j, d=d: e.activation(
                    pd[:, j * 4 + d - 1:j * 4 + d], lam[:, j:j + 1], AF.Exp,
                    scale=dlc[:]))
                tpe = p.op("scalar", lambda e, j=j, d=d: e.activation(
                    pdc[:, j:j + 1], lamc[:, j:j + 1], AF.Exp, scale=dlc[:]))
                p.wait("vector", tpe)
                p.op("vector", lambda e, j=j, d=d: e.tensor_sub(
                    pd[:, j * 4 + d - 1:j * 4 + d], pd[:, j * 4 + d - 1:j * 4 + d],
                    pdc[:, j:j + 1]))
        p.wait("tensor", tse)
        p.wait_all("tensor", "ldc")
        te2 = {}
        for j in range(4):
            if j > 0:
                p.wait("tensor", te2[j - 1])
            ttp = p.op("tensor", lambda e, j=j: e.transpose(
                psK[0:16, 0:128], e2[:, j * 16:(j + 1) * 16], idf[:]))
            p.wait("vector", ttp)
            te2[j] = p.op("vector", lambda e, j=j: e.tensor_copy(
                e2t[:, j * 128:(j + 1) * 128], psK[0:16, 0:128]))
        p.wait("tensor", te2[3])

        p.wait("vector", ("ser_scalar", p.val("ser_scalar")))

        ld_guard = None
        ld_guard_pe = None
        prev_tkbt = None
        prev_stt = {j: tmsS for j in range(4)}
        prev_oev = None
        prev_aev = None
        prev_kev = None
        prev_vev = None
        prev_wo = None
        for s in range(nsb):
            t0 = s * SBK_D
            if ld_guard is not None:
                p.wait("sync", ld_guard)
                p.wait("sync", ld_guard_pe)
            else:
                p.wait("sync", tmsk)
                p.wait("sync", tmsv)
            ldr = {}
            for j in range(4):
                ldr[j] = p.dma("sync", r_sb[:, j * SBK_D:(j + 1) * SBK_D],
                               RI[j * 128:(j + 1) * 128, t0:t0 + SBK_D], sem="ld_r")
                if s == 0:
                    p.dma("sync", k_sb[:, j * KH + 5:(j + 1) * KH],
                          KI[j * 128:(j + 1) * 128, 0:SBK_D], sem="ld_k")
                    p.dma("sync", v_sb[:, j * KH + 5:(j + 1) * KH],
                          VI[j * 128:(j + 1) * 128, 0:SBK_D], sem="ld_v")
                else:
                    p.dma("sync", k_sb[:, j * KH:(j + 1) * KH],
                          KI[j * 128:(j + 1) * 128, t0 - 5:t0 + SBK_D], sem="ld_k")
                    p.dma("sync", v_sb[:, j * KH:(j + 1) * KH],
                          VI[j * 128:(j + 1) * 128, t0 - 5:t0 + SBK_D], sem="ld_v")
                ldr[j] = p.dma("sync", g_sb[:, j * SBK_D:(j + 1) * SBK_D],
                               GI[j * 128:(j + 1) * 128, t0:t0 + SBK_D], sem="ld_g")
            p.wait_all("vector", "ld_r")
            p.wait_all("vector", "ld_k")
            p.wait_all("vector", "ld_v")
            p.wait_all("vector", "ld_g")

            if noband:
                pass
            tcv_evac = None
            tmm_band = None
            for j in range([] if noband else range(4)) if False else (range(0) if noband else range(4)):
                if tmm_band is not None:
                    p.wait("vector", tmm_band)
                for di in range(5):
                    if di == 0:
                        tm1 = p.op("vector", lambda e, j=j: e.tensor_single_scalar(
                            mtmp[:], r_sb[:, j * SBK_D:(j + 1) * SBK_D],
                            tfir[:, j:j + 1], ALU.mult))
                        ksl = k_sb[:, j * KH + 5:(j + 1) * KH]
                    else:
                        tm1 = p.op("vector", lambda e, j=j, di=di: e.tensor_single_scalar(
                            mtmp[:], r_sb[:, j * SBK_D:(j + 1) * SBK_D],
                            pd[:, j * 4 + di - 1:j * 4 + di], ALU.mult))
                        ksl = k_sb[:, j * KH + 4 - di:j * KH + 4 - di + SBK_D]
                    tm2 = p.op("vector", lambda e, j=j, di=di, ksl=ksl: e.tensor_mul(
                        mstk[:, di * SBK_D:(di + 1) * SBK_D], mtmp[:], ksl))
                for di in range(5):
                    if tcv_evac is not None:
                        p.wait("tensor", tcv_evac)
                    p.wait("tensor", tm2)
                    tmm = p.op("tensor", lambda e, di=di: e.matmul(
                        psC[:], ones[:], mstk[:, di * SBK_D:(di + 1) * SBK_D],
                        start=True, stop=True))
                    p.wait("vector", tmm)
                    tcv_evac = p.op("vector", lambda e: e.tensor_copy(cvrow[:], psC[:]))
                    p.wait("gpsimd", tcv_evac)
                    p.dma("gpsimd", CDR[:, (j * 5 + di) * SBK_D:(j * 5 + di + 1) * SBK_D],
                          cvrow[:], sem="st_cv")
                    p.wait("vector", ("st_cv", p.val("st_cv")))
                    tmm_band = tmm

            p.wait_all("tensor", "ld_k")
            p.wait_all("tensor", "ld_v")
            for cl in range(ncl):
                c16 = cl * 16
                r3 = r_sb.ap().rearrange("p (j t) -> p j t", j=4)[:, :, c16:c16 + 16]
                k3 = k_sb.ap().rearrange("p (j t) -> p j t", j=4)[:, :, 5 + c16:5 + c16 + 16]
                v3 = v_sb.ap().rearrange("p (j t) -> p j t", j=4)[:, :, 5 + c16:5 + c16 + 16]
                e13 = e1.ap().rearrange("p (j t) -> p j t", j=4)
                e1c3 = e1c.ap().rearrange("p (j t) -> p j t", j=4)
                e3c3 = e3c.ap().rearrange("p (j t) -> p j t", j=4)
                e23 = e2.ap().rearrange("p (j t) -> p j t", j=4)
                rs3 = rs.ap().rearrange("p (j t) -> p j t", j=4)
                rc3 = rc.ap().rearrange("p (j t) -> p j t", j=4)
                kc3 = kc.ap().rearrange("p (j t) -> p j t", j=4)
                kb3 = kb.ap().rearrange("p (j t) -> p j t", j=4)
                if prev_oev is not None:
                    p.wait("vector", prev_oev)
                trs = p.op("vector", lambda e, r3=r3, rs3=rs3, e13=e13: e.tensor_tensor(
                    rs3, r3, e13, ALU.mult))
                trc = p.op("vector", lambda e, r3=r3, rc3=rc3, e1c3=e1c3: e.tensor_tensor(
                    rc3, r3, e1c3, ALU.mult))
                tkc = p.op("vector", lambda e, k3=k3, kc3=kc3, e3c3=e3c3: e.tensor_tensor(
                    kc3, k3, e3c3, ALU.mult))
                if prev_kev is not None:
                    p.wait("tensor", prev_kev)
                    p.wait("tensor", prev_vev)
                for j in range(4):
                    tpk = p.op("tensor", lambda e, j=j, c16=c16: e.transpose(
                        psK[0:16, j * 128:(j + 1) * 128],
                        k_sb[:, j * KH + 5 + c16:j * KH + 5 + c16 + 16], idf[:]))
                for j in range(4):
                    tpv = p.op("tensor", lambda e, j=j, c16=c16: e.transpose(
                        psV[0:16, j * 128:(j + 1) * 128],
                        v_sb[:, j * KH + 5 + c16:j * KH + 5 + c16 + 16], idf[:]))
                p.wait("scalar", tpk)
                if prev_tkbt is not None:
                    p.wait("scalar", prev_tkbt)
                prev_kev = p.op("scalar", lambda e: e.copy(ktk[:], psK[:]))
                p.wait("vector", tpv)
                prev_vev = p.op("vector", lambda e: e.tensor_copy(vtk[:], psV[:]))
                if prev_aev is not None:
                    p.wait("tensor", prev_aev)
                p.wait("tensor", tkc)
                p.wait("tensor", trc)
                for j in range(4):
                    tam = p.op("tensor", lambda e, j=j: e.matmul(
                        psA16[:, j * 16:(j + 1) * 16], kc[:, j * 16:(j + 1) * 16],
                        rc[:, j * 16:(j + 1) * 16], start=True, stop=True))
                p.wait("vector", tam)
                prev_aev = p.op("vector", lambda e: e.select(
                    asb[:], amask[:], psA16[:], azero[:], add_drain=True))
                if prev_oev is not None:
                    p.wait("tensor", prev_oev)
                p.wait("tensor", trs)
                p.wait("tensor", prev_aev)
                p.wait("tensor", prev_vev)
                for j in range(4):
                    if prev_stt[j] is not None:
                        p.wait("tensor", prev_stt[j])
                    p.op("tensor", lambda e, j=j: e.matmul(
                        psO[:, j * 16:(j + 1) * 16], S_sb[:, j * 128:(j + 1) * 128],
                        rs[:, j * 16:(j + 1) * 16], start=True, stop=False))
                    tom = p.op("tensor", lambda e, j=j: e.matmul(
                        psO[:, j * 16:(j + 1) * 16], vtk[:, j * 128:(j + 1) * 128],
                        asb[:, j * 16:(j + 1) * 16], start=False, stop=True))
                p.wait("vector", tom)
                ob3 = obig.ap().rearrange("p (j t) -> p j t", j=4)[:, :, c16:c16 + 16]
                prev_oev = p.op("vector", lambda e, ob3=ob3: e.tensor_copy(
                    ob3, psO.ap().rearrange("p (j t) -> p j t", j=4)))
                p.wait("vector", prev_kev)
                tkbt = prev_tkbt = p.op("vector", lambda e: e.tensor_mul(kbt[:], ktk[:], e2t[:]))
                p.wait("tensor", tkbt)
                for j in range(4):
                    if prev_stt[j] is not None:
                        p.wait("tensor", prev_stt[j])
                    tkv = p.op("tensor", lambda e, j=j: e.matmul(
                        psS[:, j * 128:(j + 1) * 128], kbt[:, j * 128:(j + 1) * 128],
                        vtk[:, j * 128:(j + 1) * 128], start=True, stop=True))
                p.wait("vector", tkv)
                for j in range(4):
                    prev_stt[j] = p.op("vector", lambda e, j=j: e.scalar_tensor_tensor(
                        S_sb[:, j * 128:(j + 1) * 128], S_sb[:, j * 128:(j + 1) * 128],
                        w16[:, j:j + 1], psS[:, j * 128:(j + 1) * 128],
                        ALU.mult, ALU.add))

            p.wait_all("sync", "st_cv")
            tcb = {}
            tcrd = None
            for j in range(0 if noband else 4):
                if tcrd is not None:
                    p.wait("sync", tcrd)
                tcb[j] = p.dma("sync", cbc[:],
                               CDR[:, j * 5 * SBK_D:(j + 1) * 5 * SBK_D]
                               .broadcast_to((128, 5 * SBK_D)), sem="ld_cb")
                p.wait("vector", tcb[j])
                vsl = v_sb[:, j * KH + 5:(j + 1) * KH]
                p.op("vector", lambda e, j=j, vsl=vsl: e.tensor_mul(
                    ocor[:, j * SBK_D:(j + 1) * SBK_D], cbc[:, 0:SBK_D], vsl))
                for di in range(1, 5):
                    vsh = v_sb[:, j * KH + 4 - di:j * KH + 4 - di + SBK_D]
                    tcrd = p.op("vector", lambda e, j=j, di=di, vsh=vsh: e.tensor_mul(
                        mtmp[:], cbc[:, di * SBK_D:(di + 1) * SBK_D], vsh))
                    p.op("vector", lambda e, j=j: e.tensor_add(
                        ocor[:, j * SBK_D:(j + 1) * SBK_D],
                        ocor[:, j * SBK_D:(j + 1) * SBK_D], mtmp[:]))
                p.op("vector", lambda e, j=j: e.tensor_add(
                    obig[:, j * SBK_D:(j + 1) * SBK_D],
                    obig[:, j * SBK_D:(j + 1) * SBK_D],
                    ocor[:, j * SBK_D:(j + 1) * SBK_D]))
                tog = p.op("vector", lambda e, j=j: e.tensor_mul(
                    ogb[:, j * SBK_D:(j + 1) * SBK_D],
                    obig[:, j * SBK_D:(j + 1) * SBK_D],
                    g_sb[:, j * SBK_D:(j + 1) * SBK_D]))
            if noband:
                for j in range(4):
                    tog = p.op("vector", lambda e, j=j: e.tensor_mul(
                        ogb[:, j * SBK_D:(j + 1) * SBK_D],
                        obig[:, j * SBK_D:(j + 1) * SBK_D],
                        g_sb[:, j * SBK_D:(j + 1) * SBK_D]))
            ld_guard = tog
            ld_guard_pe = prev_vev
            p.wait_all("tensor", "ldwo")
            p.wait("tensor", tog)
            twev = {}
            for n in range(NCHUNK):
                pp = psY[n % 2]
                if prev_wo is not None and n < 2:
                    p.wait("tensor", prev_wo)
                if n >= 2:
                    p.wait("tensor", twev[n - 2])
                for j in range(4):
                    twm = p.op("tensor", lambda e, n=n, j=j, pp=pp: e.matmul(
                        pp[:], wo_sb[:, j * D + n * 128:j * D + (n + 1) * 128],
                        ogb[:, j * SBK_D:(j + 1) * SBK_D], start=(j == 0),
                        stop=(j == 3)))
                p.wait("vector", twm)
                twev[n] = p.op("vector", lambda e, pp=pp: e.tensor_copy(y1b[:], pp[:]))
                p.wait("gpsimd", twev[n])
                tsy = p.dma("gpsimd", PY1[n * 128:(n + 1) * 128, t0:t0 + SBK_D],
                            y1b[:], sem="st_y1")
                p.wait("vector", tsy)
            prev_wo = twm

        p.wait_all("gpsimd", "st_y1")
        p.realize()
    return nc


def _phase_D_inmaps(inputs, routs):
    idb = np.eye(128, dtype=bf16)
    idf = np.eye(128, dtype=np.float32)
    td = inputs["L1_attn_time_decay"]
    tf = inputs["L1_attn_time_first"]
    maps = []
    for core in range(8):
        b = core // 4
        h0 = 4 * (core % 4)
        r0 = h0 * 128
        m = {"ri": np.ascontiguousarray(routs["r"][r0:r0 + 512, b * T:(b + 1) * T]),
             "ki": np.ascontiguousarray(routs["k"][r0:r0 + 512, b * T:(b + 1) * T]),
             "vi": np.ascontiguousarray(routs["v"][r0:r0 + 512, b * T:(b + 1) * T]),
             "gi": np.ascontiguousarray(routs["g"][r0:r0 + 512, b * T:(b + 1) * T]),
             "wdec": np.ascontiguousarray(td[h0:h0 + 4].T),
             "tfir": np.ascontiguousarray(tf[h0:h0 + 4].T),
             "woT": np.ascontiguousarray(inputs["L1_attn_Wo"].T[r0:r0 + 512, :]),
             "identb": idb, "identf": idf}
        maps.append(m)
    return maps


def run_phase_D(inputs, routs, sum_partials=True):
    nc = build_phase_D()
    res = run_bass_kernel_spmd(nc, _phase_D_inmaps(inputs, routs),
                               core_ids=list(range(8)))
    if sum_partials:
        y1 = np.zeros((2, D, T), np.float32)
        for core in range(8):
            y1[core // 4] += res.results[core]["py1"]
        return y1
    return [r["py1"] for r in res.results]




TSTRIPE = 512
NSTRIPE = GT // TSTRIPE


def build_phase_E(nstripe=NSTRIPE):
    nc = bass.Bass("TRN2", target_bir_lowering=False)
    GWP = nc.dram_tensor("gwp", [2048, 1024], I32, kind="ExternalInput").ap()
    GSC = nc.dram_tensor("gsc", [2048, 64], F32, kind="ExternalInput").ap()
    UWP = nc.dram_tensor("uwp", [2048, 1024], I32, kind="ExternalInput").ap()
    USC = nc.dram_tensor("usc", [2048, 64], F32, kind="ExternalInput").ap()
    DWP = nc.dram_tensor("dwp", [2048, 1024], I32, kind="ExternalInput").ap()
    DSC = nc.dram_tensor("dsc", [2048, 64], F32, kind="ExternalInput").ap()
    XN1T = nc.dram_tensor("xn1T", [D, GT], BF16, kind="ExternalInput").ap()
    IDB = nc.dram_tensor("identb", [128, 128], BF16, kind="ExternalInput").ap()
    PFFN = nc.dram_tensor("pffn", [D, GT], F32, kind="ExternalOutput").ap()
    INTS = nc.dram_tensor("ints", [2048, GT], BF16, kind="Internal").ap()

    with contextlib.ExitStack() as stack:
        sb = lambda name, shape, dt: stack.enter_context(nc.sbuf_tensor(name, shape, dt))
        ps = lambda name, shape, dt: stack.enter_context(nc.psum_tensor(name, shape, dt))

        wbig = sb("wbig", [128, 2 * NCHUNK * D], BF16)
        pw = [sb(f"pwe{i}", [128, 1024], I32) for i in range(2)]
        scl = [sb(f"scle{i}", [128, 64], F32) for i in range(2)]
        rf = sb("rfe", [128, 1024], F32)
        s8 = sb("s8e", [128, 1024], F32)
        ta = sb("tae", [128, 1024], F32)
        tb2 = sb("tbe", [128, 1024], F32)
        t1 = sb("t1e", [128, 1024], F32)
        wq = sb("wqe", [128, D], BF16)
        xstr = [sb(f"xstr{i}", [128, NCHUNK * TSTRIPE], BF16) for i in range(1)]
        gf = sb("gfe", [128, TSTRIPE], F32)
        uf = sb("ufe", [128, TSTRIPE], F32)
        ga = sb("gae", [128, TSTRIPE], F32)
        gb = sb("gbe", [128, TSTRIPE], F32)
        ib = sb("ibe", [128, NCHUNK * TSTRIPE], BF16)
        ffb = sb("ffbe", [128, TSTRIPE], F32)
        ident = sb("idente", [128, 128], BF16)

        pst = [ps(f"pste{i}", [128, 128], BF16) for i in range(2)]
        psG = [ps(f"psGe{i}", [128, 512], F32) for i in range(2)]
        psU = [ps(f"psUe{i}", [128, 512], F32) for i in range(2)]
        psD = [ps(f"psDe{i}", [128, 512], F32) for i in range(2)]

        p = Prog(nc, stack)
        p.dma("sync", ident[:], IDB, sem="ldc")

        pst_guard = {0: None, 1: None}
        deq_guard = {0: None, 1: None}

        def dequant(WP, SC, dst_off, nrow_chunks=16):
            tr_tok = None
            for rt in range(nrow_chunks):
                bi = rt % 2
                if deq_guard[bi] is not None:
                    p.wait("sync", deq_guard[bi])
                tpw = p.dma("sync", pw[bi][:], WP[rt * 128:(rt + 1) * 128, :],
                            sem=f"ld_pw{bi}")
                tsc = p.dma("sync", scl[bi][:], SC[rt * 128:(rt + 1) * 128, :],
                            sem=f"ld_sc{bi}")
                p.wait("vector", tpw)
                p.wait("vector", tsc)
                p.wait("scalar", tpw)
                for half in range(2):
                    sh = 4 * half
                    if sh:
                        p.op("vector", lambda e, bi=bi, sh=sh: e.tensor_scalar(
                            t1.ap().bitcast(I32), pw[bi][:], sh, 7,
                            ALU.logical_shift_right, ALU.bitwise_and))
                        p.op("vector", lambda e, bi=bi, sh=sh: e.tensor_scalar(
                            ta.ap().bitcast(I32), pw[bi][:], sh, 8,
                            ALU.logical_shift_right, ALU.bitwise_and))
                    else:
                        p.op("vector", lambda e, bi=bi: e.tensor_single_scalar(
                            t1.ap().bitcast(I32), pw[bi][:], 7, ALU.bitwise_and))
                        p.op("vector", lambda e, bi=bi: e.tensor_single_scalar(
                            ta.ap().bitcast(I32), pw[bi][:], 8, ALU.bitwise_and))
                    tcv1 = p.op("vector", lambda e: e.tensor_copy(
                        rf[:], t1.ap().bitcast(I32)))
                    tcv2 = p.op("vector", lambda e: e.tensor_copy(
                        s8[:], ta.ap().bitcast(I32)))
                    p.op("vector", lambda e: e.tensor_scalar(
                        ta[:], rf[:], 4.0, 0.0, ALU.subtract, ALU.max))
                    p.op("vector", lambda e: e.tensor_scalar(
                        tb2[:], rf[:], 6.0, 0.0, ALU.subtract, ALU.max))
                    p.op("vector", lambda e: e.tensor_add(t1[:], rf[:], ta[:]))
                    p.op("vector", lambda e: e.scalar_tensor_tensor(
                        t1[:], tb2[:], 2.0, t1[:], ALU.mult, ALU.add))
                    p.op("vector", lambda e: e.tensor_scalar(
                        s8[:], s8[:], -0.125, 0.5, ALU.mult, ALU.add))
                    p.op("vector", lambda e: e.tensor_mul(t1[:], t1[:], s8[:]))
                    twq = p.op("vector", lambda e, bi=bi, half=half: e.tensor_tensor(
                        wq.ap().rearrange("p (a b) -> p a b", b=2)[:, :, half],
                        t1.ap().rearrange("p (a b) -> p a b", b=16),
                        scl[bi].ap().unsqueeze(2).broadcast_to((128, 64, 16)),
                        ALU.mult))
                deq_guard[bi] = twq
                p.wait("tensor", twq)
                for c in range(NCHUNK):
                    g = rt * NCHUNK + c
                    pt = pst[g % 2]
                    if pst_guard[g % 2] is not None:
                        p.wait("tensor", pst_guard[g % 2])
                    ttp = p.op("tensor", lambda e, pt=pt, c=c: e.transpose(
                        pt[:], wq[:, c * 128:(c + 1) * 128], ident[:]))
                    p.wait("vector", ttp)
                    tev = p.op("vector", lambda e, pt=pt, c=c, rt=rt: e.tensor_copy(
                        wbig[:, dst_off + c * 2048 + rt * 128:
                             dst_off + c * 2048 + (rt + 1) * 128], pt[:]))
                    pst_guard[g % 2] = tev
                    tr_tok = tev
            return tr_tok

        p.wait_all("tensor", "ldc")
        tg = dequant(GWP, GSC, 0)
        tu = dequant(UWP, USC, NCHUNK * D)

        c798 = sb("c798", [128, 1], F32)
        p.op("vector", lambda e: e.memset(c798[:], 0.7978845608028654))
        ev_g = {}
        ev_u = {}
        st_int = {}
        ld_x = {}
        for st in range(nstripe):
            if st >= 1:
                p.wait("sync", st_int[st - 1])
            ld_x[st] = p.dma("sync",
                             xstr[0].ap().rearrange("p (c t) -> p c t", t=TSTRIPE),
                             XN1T.rearrange("(c p) t -> p c t", p=128)
                             [:, :, st * TSTRIPE:(st + 1) * TSTRIPE],
                             sem=f"ld_x{st % 2}")
            p.wait("tensor", ld_x[st])
            if st == 0:
                p.wait("tensor", tg)
                p.wait("tensor", tu)
            for n in range(NCHUNK):
                m = st * NCHUNK + n
                pg = psG[n % 2]
                pu = psU[n % 2]
                if m >= 2:
                    p.wait("tensor", ev_u[m - 2])
                for c in range(NCHUNK):
                    p.op("tensor", lambda e, pg=pg, n=n, c=c: e.matmul(
                        pg[:], wbig[:, c * 2048 + n * 128:c * 2048 + (n + 1) * 128],
                        xstr[0][:, c * TSTRIPE:(c + 1) * TSTRIPE],
                        start=(c == 0), stop=(c == NCHUNK - 1)))
                for c in range(NCHUNK):
                    tmu = p.op("tensor", lambda e, pu=pu, n=n, c=c: e.matmul(
                        pu[:], wbig[:, NCHUNK * D + c * 2048 + n * 128:
                                    NCHUNK * D + c * 2048 + (n + 1) * 128],
                        xstr[0][:, c * TSTRIPE:(c + 1) * TSTRIPE],
                        start=(c == 0), stop=(c == NCHUNK - 1)))
                p.wait("vector", tmu)
                tgf = p.op("vector", lambda e, pg=pg: e.tensor_copy(gf[:], pg[:]))
                ev_u[m] = p.op("vector", lambda e, pu=pu: e.tensor_copy(uf[:], pu[:]))
                p.wait("scalar", tgf)
                tsqg = p.op("scalar", lambda e: e.activation(ga[:], gf[:], AF.Square))
                p.wait("vector", tsqg)
                p.op("vector", lambda e: e.tensor_scalar(
                    ga[:], ga[:], 0.044715, 1.0, ALU.mult, ALU.add))
                tgc = p.op("vector", lambda e: e.tensor_mul(ga[:], ga[:], gf[:]))
                p.wait("scalar", tgc)
                tth = p.op("scalar", lambda e: e.activation(
                    gb[:], ga[:], AF.Tanh, scale=c798[:]))
                p.wait("vector", tth)
                p.op("vector", lambda e: e.scalar_tensor_tensor(
                    gb[:], gb[:], 1.0, gf[:], ALU.add, ALU.mult))
                p.op("vector", lambda e, n=n: e.scalar_tensor_tensor(
                    ib[:, n * TSTRIPE:(n + 1) * TSTRIPE], gb[:], 0.5, uf[:],
                    ALU.mult, ALU.mult))
            p.wait("gpsimd", ("ser_vector", p.val("ser_vector")))
            st_int[st] = p.dma("gpsimd",
                               INTS.rearrange("(c p) t -> p c t", p=128)
                               [:, :, st * TSTRIPE:(st + 1) * TSTRIPE],
                               ib.ap().rearrange("p (c t) -> p c t", t=TSTRIPE),
                               sem=f"st_i{st % 2}")

        p.wait("vector", ("ser_tensor", p.val("ser_tensor")))
        twd = dequant(DWP, DSC, 0)

        p.wait_all("sync", "st_i0")
        p.wait_all("sync", "st_i1")
        ev_d = {}
        st_ff = {}
        for st in range(nstripe):
            tld = p.dma("sync", ib.ap().rearrange("p (c t) -> p c t", t=TSTRIPE),
                        INTS.rearrange("(c p) t -> p c t", p=128)
                        [:, :, st * TSTRIPE:(st + 1) * TSTRIPE],
                        sem=f"ld_i{st % 2}")
            p.wait("tensor", tld)
            if st == 0:
                p.wait("tensor", twd)
            for n in range(NCHUNK):
                m = st * NCHUNK + n
                pd_ = psD[n % 2]
                if m >= 2:
                    p.wait("tensor", ev_d[m - 2])
                for c in range(NCHUNK):
                    tmd = p.op("tensor", lambda e, pd_=pd_, n=n, c=c: e.matmul(
                        pd_[:], wbig[:, c * 2048 + n * 128:c * 2048 + (n + 1) * 128],
                        ib[:, c * TSTRIPE:(c + 1) * TSTRIPE],
                        start=(c == 0), stop=(c == NCHUNK - 1)))
                p.wait("vector", tmd)
                ev_d[m] = p.op("vector", lambda e, pd_=pd_: e.tensor_copy(ffb[:], pd_[:]))
                p.wait("gpsimd", ev_d[m])
                tsf = p.dma("gpsimd",
                            PFFN[n * 128:(n + 1) * 128, st * TSTRIPE:(st + 1) * TSTRIPE],
                            ffb[:], sem="st_ff")
                p.wait("vector", tsf)
            st_ff[st] = tsf
            p.wait("sync", ("ser_tensor", p.val("ser_tensor")))

        p.wait_all("gpsimd", "st_ff")
        p.realize()
    return nc


def _phase_E_inmaps(inputs, xn1T):
    idb = np.eye(128, dtype=bf16)
    maps = []
    xn1T = np.ascontiguousarray(xn1T)
    for c in range(8):
        n0 = c * 2048
        m = {"gwp": np.ascontiguousarray(inputs["L1_ffn_gate_W_packed"][n0:n0 + 2048]),
             "gsc": np.ascontiguousarray(inputs["L1_ffn_gate_scales"][n0:n0 + 2048]),
             "uwp": np.ascontiguousarray(inputs["L1_ffn_up_W_packed"][n0:n0 + 2048]),
             "usc": np.ascontiguousarray(inputs["L1_ffn_up_scales"][n0:n0 + 2048]),
             "dwp": np.ascontiguousarray(inputs["L1_ffn_down_W_packed"][:, c * 1024:(c + 1) * 1024]),
             "dsc": np.ascontiguousarray(inputs["L1_ffn_down_scales"][:, c * 64:(c + 1) * 64]),
             "xn1T": xn1T, "identb": idb}
        maps.append(m)
    return maps


def run_phase_E(inputs, xn1T):
    nc = build_phase_E()
    res = run_bass_kernel_spmd(nc, _phase_E_inmaps(inputs, xn1T),
                               core_ids=list(range(8)))
    return [r["pffn"] for r in res.results]




def build_phase_F():
    nc = bass.Bass("TRN2", target_bir_lowering=False)
    X1 = nc.dram_tensor("x1", [TOK, D], F32, kind="ExternalInput").ap()
    PY = [nc.dram_tensor(f"py{i}", [TOK, D], F32, kind="ExternalInput").ap()
          for i in range(4)]
    PF = [nc.dram_tensor(f"pf{i}", [TOK, D], F32, kind="ExternalInput").ap()
          for i in range(8)]
    LNW = nc.dram_tensor("lnw", [1, D], F32, kind="ExternalInput").ap()
    LNB = nc.dram_tensor("lnb", [1, D], F32, kind="ExternalInput").ap()
    OUT = nc.dram_tensor("out", [TOK, D], F32, kind="ExternalOutput").ap()

    NT = TOK // 128

    with contextlib.ExitStack() as stack:
        sb = lambda name, shape, dt: stack.enter_context(nc.sbuf_tensor(name, shape, dt))

        x1b = sb("x1bf", [128, D], F32)
        pyb = [sb(f"pybf{i}", [128, D], F32) for i in range(4)]
        pfb = [sb(f"pfbf{i}", [128, D], F32) for i in range(8)]
        y1t = sb("y1tf", [128, D], F32)
        att = sb("attf", [128, D], F32)
        lnwb = sb("lnwbf", [128, D], F32)
        lnbb = sb("lnbbf", [128, D], F32)
        mu = sb("muf", [128, 1], F32)
        mur = sb("murf", [128, 1], F32)
        sqa = sb("sqaf", [128, 1], F32)
        var = sb("varf", [128, 1], F32)
        sd = sb("sdf", [128, 1], F32)
        rstd = sb("rstdf", [128, 1], F32)
        t1 = sb("t1f", [128, 1], F32)
        epsc = sb("epscf", [128, 1], F32)

        p = Prog(nc, stack)
        p.dma("sync", lnwb[:], LNW.broadcast_to((128, D)), sem="ldc")
        p.dma("sync", lnbb[:], LNB.broadcast_to((128, D)), sem="ldc")
        p.op("vector", lambda e: e.memset(epsc[:], 1e-5))
        p.wait_all("vector", "ldc")

        tdone = {}
        tsq_t = {}
        for i in range(NT):
            sl = slice(i * 128, (i + 1) * 128)
            if i >= 1:
                p.wait("sync", tdone[i - 1])
                p.wait("sync", tsq_t[i - 1])
            toks = [p.dma("sync", x1b[:], X1[sl, :], sem="ld_a")]
            for j in range(4):
                toks.append(p.dma("sync", pyb[j][:], PY[j][sl, :], sem=f"ld_b{j}"))
            for j in range(8):
                toks.append(p.dma("sync", pfb[j][:], PF[j][sl, :], sem=f"ld_c{j}"))
            for tk in toks:
                p.wait("vector", tk)
            p.op("vector", lambda e: e.tensor_add(y1t[:], pyb[0][:], pyb[1][:]))
            p.op("vector", lambda e: e.tensor_add(y1t[:], y1t[:], pyb[2][:]))
            ty1 = p.op("vector", lambda e: e.tensor_add(y1t[:], y1t[:], pyb[3][:]))
            p.op("vector", lambda e: e.reduce_sum(mur[:], y1t[:], axis=AX.X))
            p.op("vector", lambda e: e.tensor_scalar_mul(mu[:], mur[:], 1.0 / D))
            p.wait("scalar", ty1)
            tsq = tsq_t[i] = p.op("scalar", lambda e: e.activation(
                att[:], y1t[:], AF.Square, accum_out=sqa[:]))
            p.op("vector", lambda e: e.tensor_mul(t1[:], mu[:], mu[:]))
            p.wait("vector", tsq)
            p.op("vector", lambda e: e.tensor_scalar_mul(var[:], sqa[:], 1.0 / D))
            tvar = p.op("vector", lambda e: e.tensor_sub(var[:], var[:], t1[:]))
            p.wait("scalar", tvar)
            tsd = p.op("scalar", lambda e: e.activation(sd[:], var[:], AF.Sqrt,
                                                        bias=epsc[:]))
            p.wait("vector", tsd)
            p.op("vector", lambda e: e.reciprocal(rstd[:], sd[:]))
            p.op("vector", lambda e: e.scalar_tensor_tensor(
                att[:], y1t[:], mu[:], rstd[:].broadcast_to((128, D)),
                ALU.subtract, ALU.mult))
            p.op("vector", lambda e: e.tensor_mul(att[:], att[:], lnwb[:]))
            p.op("vector", lambda e: e.tensor_add(att[:], att[:], lnbb[:]))
            p.op("vector", lambda e: e.tensor_add(att[:], att[:], x1b[:]))
            for j in range(8):
                tlast = p.op("vector", lambda e, j=j: e.tensor_add(
                    att[:], att[:], pfb[j][:]))
            p.wait("gpsimd", tlast)
            tst = p.dma("gpsimd", OUT[sl, :], att[:], sem="st_o")
            p.wait("vector", tst)
            p.wait("scalar", tst)
            tdone[i] = tst

        p.wait_all("gpsimd", "st_o")
        p.realize()
    return nc


def kernel(**inputs):
    inputs = {k: np.asarray(v) for k, v in inputs.items()}
    xn, delta, bc, ffn0 = run_phase_A(inputs)
    y = run_phase_B(inputs, xn, delta, bc)
    y_rows = np.ascontiguousarray(
        np.concatenate([y[0].T, y[1].T], axis=0))
    f0_rows = np.ascontiguousarray(ffn0.T)
    x1, xn1T, outs = run_phase_C(inputs, y_rows, f0_rows)
    py1s = run_phase_D(inputs, outs, sum_partials=False)
    pffn = run_phase_E(inputs, xn1T)
    nc = build_phase_F()
    lnw = np.ascontiguousarray(inputs["L1_attn_ln_w"].reshape(1, D))
    lnb = np.ascontiguousarray(inputs["L1_attn_ln_b"].reshape(1, D))
    py_rows = [np.ascontiguousarray(p_.T) for p_ in py1s]
    pf_rows = [np.ascontiguousarray(pf.T) for pf in pffn]
    maps = []
    for c in range(8):
        b = c // 4
        sl = slice(c * TOK, (c + 1) * TOK)
        tsl = slice((c % 4) * TOK, (c % 4 + 1) * TOK)
        m = {"x1": np.ascontiguousarray(x1[sl]), "lnw": lnw, "lnb": lnb}
        for j in range(4):
            m[f"py{j}"] = np.ascontiguousarray(py_rows[4 * b + j][tsl])
        for j in range(8):
            m[f"pf{j}"] = np.ascontiguousarray(pf_rows[j][sl])
        maps.append(m)
    res = run_bass_kernel_spmd(nc, maps, core_ids=list(range(8)))
    out = np.concatenate([r["out"] for r in res.results], axis=0)
    return out.reshape(B, T, D)

